# version 06
# speedup vs baseline: 1.1920x; 1.0202x over previous
import numpy as np
import ml_dtypes
import concourse.bass as bass
import concourse.mybir as mybir
from concourse.bass_utils import run_bass_kernel_spmd

BFNP = ml_dtypes.bfloat16
F32 = mybir.dt.float32
BF16 = mybir.dt.bfloat16
AF = mybir.ActivationFunctionType
ALU = mybir.AluOpType
AX = mybir.AxisListType.X

T = 2048
D = 1024
NB = 2
DH = 4096
NCORES = 8


def _l0_perm():
    p = np.zeros(D, dtype=np.int64)
    for h in range(16):
        for j in range(32):
            p[64 * h + j] = 64 * h + 2 * j
            p[64 * h + 32 + j] = 64 * h + 2 * j + 1
    return p

def _l1_perm(nheads):
    p = np.zeros(64 * nheads, dtype=np.int64)
    for h in range(nheads):
        b = 64 * h
        for j in range(16):
            p[b + j] = b + 2 * j
            p[b + 16 + j] = b + 2 * j + 1
            p[b + 32 + j] = b + 32 + 2 * j
            p[b + 48 + j] = b + 32 + 2 * j + 1
    return p

def _rope_l0():
    j = np.arange(32)
    invf = 10000.0 ** (-2.0 * j / 64.0)
    ang = np.arange(T, dtype=np.float64)[None, :] * invf[:, None]
    cc = np.tile(np.cos(ang), (4, 1))
    ss = np.tile(np.sin(ang), (4, 1))
    return cc.astype(BFNP), ss.astype(BFNP)

def _rope_l1(scale):
    j = np.arange(16)
    invf = 10000.0 ** (-2.0 * j / 32.0)
    ang = np.arange(T, dtype=np.float64)[None, :] * invf[:, None]
    cc = np.tile(np.cos(ang), (8, 1)) * scale
    ss = np.tile(np.sin(ang), (8, 1)) * scale
    return cc.astype(BFNP), ss.astype(BFNP)

def _p_swap(block):
    P = np.zeros((128, 128), dtype=BFNP)
    for base in range(0, 128, 2 * block):
        for j in range(block):
            P[base + j, base + block + j] = 1.0
            P[base + block + j, base + j] = -1.0
    return P

def _bfly_vectors(w):
    out = np.zeros((12, 3, DH), dtype=np.float32)
    pos = np.arange(DH)
    for s in range(12):
        st = 2 ** s
        if s < 6:
            logical = pos
        else:
            logical = (pos % 64) * 64 + pos // 64
        nb_idx = logical // (2 * st)
        jj = logical % st
        isb = (logical & st) != 0
        wr = w[s].reshape(DH // (2 * st), st, 2, 2)
        w00 = wr[nb_idx, jj, 0, 0]; w01 = wr[nb_idx, jj, 0, 1]
        w10 = wr[nb_idx, jj, 1, 0]; w11 = wr[nb_idx, jj, 1, 1]
        wd = np.where(isb, w11, w00)
        wo = np.where(isb, w10, w01)
        out[s, 0] = wd
        out[s, 1] = wo
        lbit = st if s < 6 else 2 ** (s - 6)
        out[s, 2] = wo[pos ^ lbit]
    return out

def _stage_partner_layout(s):
    return s if s < 6 else s - 6

def host_prep(W):
    p0 = _l0_perm()
    pq = _l1_perm(16)
    pk = _l1_perm(4)
    d = {}
    d['wq0'] = np.ascontiguousarray(W['L0_attn_wq'][:, p0])
    d['wk0'] = np.ascontiguousarray(W['L0_attn_wk'][:, p0])
    d['wv0'] = np.ascontiguousarray(W['L0_attn_wv'])
    d['wo0'] = np.ascontiguousarray(W['L0_attn_wo'])
    d['wexp'] = np.ascontiguousarray(W['L0_ffn_W_exp'])
    pa = np.arange(DH)
    permA = (pa % 64) * 64 + pa // 64
    d['wcon'] = np.ascontiguousarray(W['L0_ffn_W_con'][permA, :])
    d['b_exp'] = np.ascontiguousarray(W['L0_ffn_b_exp'].reshape(32, 128).T)
    d['up_bias_A'] = np.ascontiguousarray(
        W['L0_ffn_up_bias'][permA].reshape(32, 128).T)
    d['b_con'] = np.ascontiguousarray(W['L0_ffn_b_con'].reshape(8, 128).T)
    d['down_bias'] = np.ascontiguousarray(W['L0_ffn_down_bias'].reshape(8, 128).T)
    d['bfw_up'] = _bfly_vectors(np.asarray(W['L0_ffn_up_weights']))
    d['bfw_dn'] = _bfly_vectors(np.asarray(W['L0_ffn_down_weights']))
    for nm in ('bfw_up', 'bfw_dn'):
        v = d[nm]
        bc = np.empty((12, 2, 64, DH), dtype=np.float32)
        bc[:, 0] = np.repeat(v[:, 0][:, None, :], 64, axis=1)
        bc[:, 1] = np.repeat(v[:, 2][:, None, :], 64, axis=1)
        d[nm + '_bc'] = np.ascontiguousarray(bc)
    d['wq1'] = np.ascontiguousarray(W['L1_attn_Wq'][pq, :].T)
    d['wk1'] = np.ascontiguousarray(W['L1_attn_Wkv'][0:256][pk, :].T)
    d['wv1'] = np.ascontiguousarray(W['L1_attn_Wkv'][256:512].T)
    d['wout1'] = np.ascontiguousarray(W['L1_attn_Wout'].T)
    d['lam'] = np.ascontiguousarray(W['L1_attn_lambda_init'].reshape(1, 16))
    d['rmsw'] = np.ascontiguousarray(W['L1_attn_rms_weight'].reshape(64, 1))
    d['wgate'] = np.ascontiguousarray(W['L1_ffn_w_gate'].T)
    d['w1e'] = np.ascontiguousarray(np.transpose(W['L1_ffn_w1_e'], (0, 2, 1)))
    d['w3e'] = np.ascontiguousarray(np.transpose(W['L1_ffn_w3_e'], (0, 2, 1)))
    d['w2e'] = np.ascontiguousarray(np.transpose(W['L1_ffn_w2_e'], (0, 2, 1)))
    d['w1s'] = np.ascontiguousarray(W['L1_ffn_w1_s'].T)
    d['w3s'] = np.ascontiguousarray(W['L1_ffn_w3_s'].T)
    d['w2s'] = np.ascontiguousarray(W['L1_ffn_w2_s'].T)
    cc0, ss0 = _rope_l0()
    cq, sq = _rope_l1(1.0 / np.sqrt(32.0))
    ck, sk = _rope_l1(1.0)
    d['cc0'], d['ss0'] = cc0, ss0
    d['cc1q'], d['ss1q'] = cq, sq
    d['cc1k'], d['ss1k'] = ck, sk
    d['p32'] = _p_swap(32)
    d['p16'] = _p_swap(16)
    d['eye_bf'] = np.eye(128, dtype=BFNP)
    d['eye_f32'] = np.eye(128, dtype=np.float32)
    tri = np.tril(np.ones((128, 128), dtype=np.float32)).T
    d['tri4'] = np.ascontiguousarray(np.tile(tri, (1, 4)).astype(BFNP))
    kk = np.arange(128)[:, None]
    cc_ = np.arange(1024)[None, :]
    d['stair'] = np.ascontiguousarray((cc_ >= kk + 384).astype(BFNP))
    ii = np.arange(DH)
    d['x0c'] = np.ascontiguousarray(
        (ii[None, :] % 64 == np.arange(64)[:, None]).astype(BFNP))
    return d


ENGINES = ('sync', 'tensor', 'vector', 'scalar', 'gpsimd')

class Res:
    __slots__ = ('w', 'r', 'name')
    def __init__(self, name=''):
        self.w = None
        self.r = []
        self.name = name

class B:
    def __init__(self, nc):
        self.nc = nc
        self.q = {e: [] for e in ENGINES}
        self.cnt = {}
        self.sems = {}

    def _c(self, nm, k):
        self.cnt[nm] = self.cnt.get(nm, 0) + k
        return self.cnt[nm]

    def op(self, eng, fn, reads=(), writes=(), extra_wait=(), dma=False,
           sem=None):
        sem = sem or ('D' + eng[0] if dma else eng[0].upper())
        inc = 16 if dma else 1
        waits = {}
        if dma:
            waits[sem] = self.cnt.get(sem, 0)
        for r in reads:
            if r.w:
                waits[r.w[0]] = max(waits.get(r.w[0], 0), r.w[1])
        for r in writes:
            if r.w:
                waits[r.w[0]] = max(waits.get(r.w[0], 0), r.w[1])
            for s_v in r.r:
                waits[s_v[0]] = max(waits.get(s_v[0], 0), s_v[1])
        for s, v in extra_wait:
            waits[s] = max(waits.get(s, 0), v)
        val = self._c(sem, inc)
        wl = tuple(waits.items())
        def run(e, _fn=fn, _wl=wl, _sem=sem, _inc=inc):
            for nm, v in _wl:
                e.wait_ge(self.sems[nm], v)
            inst = _fn(e)
            inst.then_inc(self.sems[_sem], _inc)
        self.q[eng].append(run)
        tok = (sem, val)
        for r in writes:
            r.w = tok
            r.r = []
        for r in reads:
            r.r.append(tok)
        return tok

    def barrier_wait(self, eng, toks):
        toks = [t for t in toks if t]
        if not toks:
            return
        def run(e, _t=tuple(toks)):
            for nm, v in _t:
                e.wait_ge(self.sems[nm], v)
        self.q[eng].append(run)

    def finalize(self, final_waits):
        nc = self.nc
        sem_names = sorted(set(list(self.cnt.keys())
                               + [s for s, _ in final_waits]))
        import contextlib
        with contextlib.ExitStack() as st:
            for nm in sem_names:
                self.sems[nm] = st.enter_context(nc.semaphore(nm))
            with nc.Block() as block:
                for eng in ENGINES:
                    ops = self.q[eng]
                    if eng == 'sync' and final_waits:
                        def tail(e, _fw=tuple(final_waits)):
                            for nm, v in _fw:
                                e.wait_ge(self.sems[nm], v)
                        ops = ops + [tail]
                    if not ops:
                        continue
                    def body(engine, _ops=tuple(ops)):
                        for f in _ops:
                            f(engine)
                    getattr(block, eng)(body)


class Kern:

    def __init__(self, blocks=('l0a', 'l0f', 'l1a', 'l1m'), dbg=None):
        self.blocks = blocks
        self.dbg = dbg
        nc = bass.Bass('TRN2', target_bir_lowering=False)
        self.nc = nc
        self.b = B(nc)
        self._declare_io()
        self._alloc()
        self._const_views()

    def _declare_io(self):
        nc = self.nc
        self.hx = nc.dram_tensor('x', [NB, D, T], F32, kind='ExternalInput').ap()
        self.hy = nc.dram_tensor('y', [NB, D, T], F32, kind='ExternalOutput').ap()
        self.hw = {}
        shapes = dict(
            wq0=(D, D), wk0=(D, D), wv0=(D, D), wo0=(D, D),
            wexp=(D, DH), wcon=(DH, D), b_exp=(128, 32), up_bias_A=(128, 32),
            b_con=(128, 8), down_bias=(128, 8),
            bfw_up=(12, 3, DH), bfw_dn=(12, 3, DH),
            bfw_up_bc=(12, 2, 64, DH), bfw_dn_bc=(12, 2, 64, DH),
            wq1=(D, D), wk1=(D, 256), wv1=(D, 256), wout1=(D, D),
            lam=(1, 16), rmsw=(64, 1),
            wgate=(D, 16), w1e=(16, D, 512), w3e=(16, D, 512),
            w2e=(16, 512, D), w1s=(D, D), w3s=(D, D), w2s=(D, D),
        )
        consts = dict(cc0=(128, T), ss0=(128, T), cc1q=(128, T), ss1q=(128, T),
                      cc1k=(128, T), ss1k=(128, T), p32=(128, 128),
                      p16=(128, 128), eye_bf=(128, 128), eye_f32=(128, 128),
                      tri4=(128, 512), stair=(128, 1024), x0c=(64, DH))
        self.const_dt = {k: (BF16 if k != 'eye_f32' and k != 'x0c_f32' else F32)
                         for k in consts}
        for k, s in shapes.items():
            self.hw[k] = nc.dram_tensor(k, list(s), F32, kind='ExternalInput').ap()
        for k, s in consts.items():
            dt = F32 if k == 'eye_f32' else BF16
            self.hw[k] = nc.dram_tensor(k, list(s), dt, kind='ExternalInput').ap()
        self.hscr = nc.dram_tensor('hscr', [DH, 512], BF16, kind='Internal').ap()
        self.yscr = nc.dram_tensor('yscr', [D, T], BF16, kind='Internal').ap()
        self.cscr = nc.dram_tensor('cscr', [16, T], F32, kind='Internal').ap()
        self.bdscr = nc.dram_tensor('bdscr', [128, 16384], BF16,
                                    kind='Internal').ap()

    def _alloc(self):
        nc = self.nc
        A = nc.alloc_sbuf_tensor
        self.xt = A('xt', [128, 8, T], F32)
        self.bufb = A('bufb', [128, 8 * T], BF16)
        self.bufc = A('bufc', [128, 8 * T], BF16)
        self.bufd = A('bufd', [128, 8 * T], BF16)
        self.bufdf = nc.alloc_sbuf_tensor_at(
            'bufdf', [128, 4096], F32, offset=self.nc.lookup_mloc(
                self.bufd).addr)
        self.xc = A('xc', [128, 4096], BF16)
        self.scr = A('scr', [128, 4096], BF16)
        self.scf = A('scf', [128, 2560], F32)
        self.rows1 = A('rows1', [1, 1536], F32)
        self.bias_sb = A('bias_sb', [128, 96], F32)
        self.csts = {}
        for k in ('cc0', 'ss0', 'cc1q', 'ss1q', 'cc1k', 'ss1k'):
            pass
        self.cst = A('cst', [128, 2368], BF16)
        self.cstf = A('cstf', [128, 288], F32)
        self.pb = [nc.alloc_psum_tensor('pb%d' % i, [128, 512], F32)
                   for i in range(6)]
        self.pt = [nc.alloc_psum_tensor('pt%d' % i, [128, 1024], BF16)
                   for i in range(2)]
        self.r = {n: Res(n) for n in
                  ('x', 'b', 'c', 'd', 'xc', 'scr', 'scf', 'rows',
                   'cst', 'cstf', 'bias', 'hscr', 'bdscr', 'yscr', 'cscr')}
        self.rpb = [Res('pb%d' % i) for i in range(6)]
        self.rpt = [Res('pt%d' % i) for i in range(2)]
        self.rpan = [Res('pan0'), Res('pan1')]
        self.rsl = [Res('msl%d' % i) for i in range(4)]
        self.rP = [Res('mP0'), Res('mP1')]
        self.re = [Res('e%d' % i) for i in range(4)]
        self.rps = [[Res('ps%d%d' % (p, s)) for s in range(2)]
                    for p in range(2)]

    def merge_aliases(self):
        groups = ((self.r['d'], self.rpan[0], self.rpan[1],
                   *self.rps[0], *self.rps[1]),
                  (self.r['c'], *self.rsl),
                  (self.r['scr'], *self.rP, *self.re))
        for g in groups:
            best = {}
            for m in g:
                for s, v in m.r + ([m.w] if m.w else []):
                    best[s] = max(best.get(s, 0), v)
            toks = list(best.items())
            for m in g:
                m.r = list(toks)

    def _const_views(self):
        c = self.cst
        self.v_p32 = c[:, 0:128]
        self.v_p16 = c[:, 128:256]
        self.v_eye = c[:, 256:384]
        self.v_tri4 = c[:, 384:896]
        self.v_stair = c[:, 896:1920]
        self.v_ones_bf = c[:, 1920:1984]
        self.v_onesrow = c[0:1, 1984:2112]
        self.v_eyef = self.cstf[:, 0:128]
        self.v_onesrowf = self.cstf[0:1, 128:256]
        self.v_colf = self.cstf[:, 256:288]

    def loadw(self, name, sl, out_ap, res, eng='gpsimd'):
        hw = self.hw[name]
        src = hw if sl is None else hw[sl]
        return self.b.op(eng, lambda e, _s=src, _o=out_ap: e.dma_start(
            out=_o, in_=_s), reads=(), writes=(res,), dma=True)

    def emit_const_load(self):
        b = self.b
        for nm, view in (('p32', self.v_p32), ('p16', self.v_p16),
                         ('eye_bf', self.v_eye), ('tri4', self.v_tri4),
                         ('stair', self.v_stair)):
            b.op('sync', lambda e, _n=nm, _v=view: e.dma_start(
                out=_v, in_=self.hw[_n]), writes=(self.r['cst'],), dma=True)
        b.op('sync', lambda e: e.dma_start(out=self.v_eyef, in_=self.hw['eye_f32']),
             writes=(self.r['cstf'],), dma=True)
        b.op('vector', lambda e: e.memset(self.v_ones_bf, 1.0),
             writes=(self.r['cst'],))
        b.op('vector', lambda e: e.memset(self.cst[0:1, 1984:2112], 1.0),
             writes=(self.r['cst'],))
        b.op('vector', lambda e: e.memset(self.cstf[0:1, 128:256], 1.0),
             writes=(self.r['cstf'],))
        b.op('vector', lambda e: e.memset(self.v_colf[:, 0:1], 1e-5),
             writes=(self.r['cstf'],))
        b.op('vector', lambda e: e.memset(self.v_colf[:, 1:2], 1e-6),
             writes=(self.r['cstf'],))
        b.op('vector', lambda e: e.memset(self.v_colf[:, 2:3], 0.0),
             writes=(self.r['cstf'],))

    def load_x(self, bi):
        return self.b.op('sync', lambda e: e.dma_start(
            out=self.xt[:, :, :], in_=self.hx[bi].rearrange(
                '(n p) t -> p n t', p=128)),
            writes=(self.r['x'],), dma=True)

    def store_x(self, bi):
        return self.b.op('sync', lambda e: e.dma_start(
            out=self.hy[bi].rearrange('(n p) t -> p n t', p=128),
            in_=self.xt[:, :, :]), reads=(self.r['x'],), dma=True)

    def dump_bf(self, buf, res, ntiles=8):
        def cp(e):
            for dt_ in range(ntiles):
                i = e.tensor_copy(self.xt[:, dt_, :], buf[:, bass.ts(dt_, T)])
            return i
        self.b.op('vector', cp, reads=(res,), writes=(self.r['x'],))

    def ln_chunk(self, sv, s_res):
        b = self.b
        rows, rrows = self.rows1, self.r['rows']
        pbS, rpbS = self.pb[4], self.rpb[4]
        pbQ, rpbQ = self.pb[5], self.rpb[5]
        xsq, rscr = self.scr[:, 0:2048], self.r['scr']
        rbc = self.scr[:, 2048:2560]
        nbc = self.scr[:, 2560:3072]
        for half in range(2):
            def sqh(e, _h=half):
                for k in range(4):
                    i = e.activation(xsq[:, bass.ts(k, 512)],
                                     sv(4 * _h + k), AF.Square)
                return i
            b.op('scalar', sqh, reads=(s_res,), writes=(rscr,))
            def mmsq(e, _h=half):
                for k in range(4):
                    i = e.matmul(pbQ[0:1, 0:512], self.v_ones_bf[:, 0:1],
                                 xsq[:, bass.ts(k, 512)],
                                 start=(_h == 0 and k == 0),
                                 stop=(_h == 1 and k == 3))
                return i
            b.op('tensor', mmsq, reads=(rscr, self.r['cst']), writes=(rpbQ,))
        def mmsum(e):
            for dt_ in range(8):
                i = e.matmul(pbS[0:1, 0:512], self.v_ones_bf[:, 0:1],
                             sv(dt_), start=(dt_ == 0), stop=(dt_ == 7))
            return i
        b.op('tensor', mmsum, reads=(s_res, self.r['cst']), writes=(rpbS,))
        b.op('vector', lambda e: e.tensor_copy(rows[0:1, 0:512],
                                               pbS[0:1, 0:512]),
             reads=(rpbS,), writes=(rrows,))
        b.op('vector', lambda e: e.tensor_copy(rows[0:1, 512:1024],
                                               pbQ[0:1, 0:512]),
             reads=(rpbQ,), writes=(rrows,))
        ra, rb_, rc_ = rows[0:1, 0:512], rows[0:1, 512:1024], rows[0:1, 1024:1536]
        def rowm1(e):
            e.tensor_scalar(ra, ra, 1.0 / D, None, ALU.mult)
            return e.tensor_scalar(rb_, rb_, 1.0 / D, None, ALU.mult)
        b.op('vector', rowm1, reads=(rrows,), writes=(rrows,))
        b.op('vector', lambda e: e.tensor_tensor(rc_, ra, ra, ALU.mult),
             reads=(rrows,), writes=(rrows,))
        b.op('vector', lambda e: e.tensor_tensor(rb_, rb_, rc_, ALU.subtract),
             reads=(rrows,), writes=(rrows,))
        b.op('scalar', lambda e: e.activation(rc_, rb_, AF.Ln,
                                              bias=self.v_colf[0:1, 0:1],
                                              scale=1.0),
             reads=(rrows, self.r['cstf']), writes=(rrows,))
        b.op('scalar', lambda e: e.activation(rc_, rc_, AF.Exp, scale=-0.5),
             reads=(rrows,), writes=(rrows,))
        b.op('vector', lambda e: e.scalar_tensor_tensor(
            ra, ra, -1.0, rc_, ALU.mult, ALU.mult),
            reads=(rrows,), writes=(rrows,))
        b.op('tensor', lambda e: e.matmul(pbS[:, 0:512], self.v_onesrowf, rc_,
                                          start=True, stop=True),
             reads=(rrows, self.r['cstf']), writes=(rpbS,))
        b.op('tensor', lambda e: e.matmul(pbQ[:, 0:512], self.v_onesrowf, ra,
                                          start=True, stop=True),
             reads=(rrows, self.r['cstf']), writes=(rpbQ,))
        b.op('vector', lambda e: e.tensor_copy(rbc, pbS[:, 0:512]),
             reads=(rpbS,), writes=(rscr,))
        b.op('vector', lambda e: e.tensor_copy(nbc, pbQ[:, 0:512]),
             reads=(rpbQ,), writes=(rscr,))
        return rbc, nbc

    def ln_pre_chunk(self, c4, get_src, src_res, dst_view, dst_res):
        b = self.b
        cs = slice(512 * c4, 512 * c4 + 512)
        def cast(e, _cs=cs):
            for dt_ in range(8):
                i = e.tensor_copy(self.xc[:, bass.ts(dt_, 512)],
                                  get_src(dt_)[:, _cs])
            return i
        b.op('vector', cast, reads=(src_res,), writes=(self.r['xc'],))
        sv = lambda dt_: self.xc[:, bass.ts(dt_, 512)]
        rbc, nbc = self.ln_chunk(sv, self.r['xc'])
        if self.dbg == 'f19':
            _oldapp_marker = True
        if self.dbg == 'f18':
            def dmp(e):
                e.tensor_copy(self.xt[0:1, 0, 0:1536], self.rows1[0:1, :])
                e.tensor_copy(self.xt[:, 1, 0:512], rbc)
                e.tensor_copy(self.xt[:, 2, 0:512], nbc)
                return e.tensor_copy(self.xt[:, 3, 0:512],
                                     self.xc[:, 0:512])
            self.b.op('vector', dmp,
                      reads=(self.r['rows'], self.r['scr'], self.r['xc']),
                      writes=(self.r['x'],))
            return
        def app1(e):
            for dt_ in range(8):
                i = e.tensor_tensor(dst_view(dt_), sv(dt_), rbc, ALU.mult)
            return i
        b.op('vector', app1, reads=(self.r['xc'], self.r['scr']),
             writes=(dst_res,))
        def app2(e):
            for dt_ in range(8):
                i = e.tensor_tensor(dst_view(dt_), dst_view(dt_), nbc, ALU.add)
            return i
        b.op('vector', app2, reads=(self.r['scr'],), writes=(dst_res,))
        if self.dbg == 'f19':
            def dmp(e):
                e.tensor_copy(self.xt[:, 1, 0:512], self.scr[:, 2048:2560])
                e.tensor_copy(self.xt[:, 2, 0:512], self.scr[:, 2560:3072])
                return e.tensor_copy(self.xt[:, 3, 0:512],
                                     self.xc[:, 0:512])
            self.b.op('vector', dmp,
                      reads=(self.r['scr'], self.r['xc']),
                      writes=(self.r['x'],))

    def ln_full(self, get_src, src_res, dst_bf, dst_res, src_is_bf=False):
        for c4 in range(4):
            self.ln_pre_chunk(
                c4, get_src, src_res,
                lambda dt_, _c=c4: dst_bf[:, dt_ * T + 512 * _c:
                                          dt_ * T + 512 * _c + 512],
                dst_res)

    def ln_resid_chunk(self, c4, sv, s_res):
        b = self.b
        cs = slice(512 * c4, 512 * c4 + 512)
        rbc, nbc = self.ln_chunk(sv, s_res)
        def app1(e):
            for dt_ in range(8):
                i = e.tensor_tensor(sv(dt_), sv(dt_), rbc, ALU.mult)
            return i
        b.op('vector', app1, reads=(self.r['scr'],), writes=(s_res,))
        def app2(e):
            for dt_ in range(8):
                i = e.tensor_tensor(sv(dt_), sv(dt_), nbc, ALU.add)
            return i
        b.op('vector', app2, reads=(self.r['scr'],), writes=(s_res,))
        def app3(e, _cs=cs):
            for dt_ in range(8):
                i = e.tensor_tensor(self.xt[:, dt_, _cs],
                                    self.xt[:, dt_, _cs], sv(dt_), ALU.add)
            return i
        b.op('vector', app3, reads=(s_res, self.r['x']),
             writes=(self.r['x'],))

    def emit_l0_attn(self, bi):
        self.merge_aliases()
        b, r = self.b, self.r
        xt, bufb, bufc, bufd = self.xt, self.bufb, self.bufc, self.bufd
        scr, scf = self.scr, self.scf
        QT = bufc[:, 0:4096]
        KT = bufc[:, 4096:8192]
        CC = bufc[:, 8192:10240]
        SS = bufc[:, 10240:12288]
        VA = bufc[:, 12288:13328]
        KTOK = bufc[:, 13376:14400]
        SBF = bufc[:, 14400:15440]
        Sf = scf[:, 0:520]
        den_sb = scf[0:1, 576:1088]
        qraw = scr[:, 0:512]
        tmp1 = scr[:, 512:1024]
        tmp2 = scr[:, 1024:1536]
        am_bf = scr[:, 1536:2048]
        nums_bf = scr[:, 2048:2560]
        dbc_bf = scr[:, 2560:3072]
        rb, rc, rd = r['b'], r['c'], r['d']
        rscr, rscf = r['scr'], r['scf']
        pb, rpb = self.pb, self.rpb
        pt0, rpt0 = self.pt[0], self.rpt[0]

        self.ln_full(lambda dt_: xt[:, dt_, :], r['x'], bufb, rb)
        if self.dbg == 'ln':
            self.dump_bf(bufb, rb)
            return
        b.op('sync', lambda e: e.dma_start(out=CC, in_=self.hw['cc0']),
             writes=(rc,), dma=True)
        b.op('sync', lambda e: e.dma_start(out=SS, in_=self.hw['ss0']),
             writes=(rc,), dma=True)
        WV = bufd[:, 8192:16384]
        WVv = WV.rearrange('p (n o) -> p n o', o=1024)
        b.op('gpsimd', lambda e: e.dma_start(
            out=WVv, in_=self.hw['wv0'].rearrange('(n p) o -> p n o', p=128)),
            writes=(rd,), dma=True)
        b.op('vector', lambda e: e.memset(Sf, 0.0), writes=(rscf,))
        b.op('vector', lambda e: e.memset(SBF, 0.0), writes=(rc,))
        b.op('vector', lambda e: e.memset(VA, 0.0), writes=(rc,))
        b.op('vector', lambda e: e.memset(
            VA.rearrange('p (h c) -> p h c', c=65)[:, :, 64:65], 1.0),
            writes=(rc,))

        QTv = QT.rearrange('p (n o) -> p n o', o=512)
        KTv = KT.rearrange('p (n o) -> p n o', o=512)
        SBFv = SBF.rearrange('p (h c) -> p h c', c=65)
        Sfv = Sf.rearrange('p (h c) -> p h c', c=65)
        panes = [bufd[:, 0:4096].rearrange('p (n o) -> p n o', o=512),
                 bufd[:, 4096:8192].rearrange('p (n o) -> p n o', o=512)]
        rpan = self.rpan
        CCv = CC
        n_sc = 1 if self.dbg in ('proj1', 'chunk1') else 4
        for sc in range(n_sc):
            scs = slice(512 * sc, 512 * sc + 512)
            for mi, (wname, dst) in enumerate((('wq0', QT), ('wk0', KT))):
                for ph in range(2):
                    pan, rp = panes[(2 * mi + ph) % 2], rpan[(2 * mi + ph) % 2]
                    self.b.op('gpsimd', lambda e, _w=wname, _p=ph, _pan=pan:
                              e.dma_start(out=_pan, in_=self.hw[_w][
                                  :, 512 * _p:512 * _p + 512].rearrange(
                                      '(n p) o -> p n o', p=128)),
                              writes=(rp,), dma=True)
                    for d4 in range(4):
                        dout = 4 * ph + d4
                        def mmq(e, _pan=pan, _d4=d4, _scs=scs):
                            for di in range(8):
                                i = e.matmul(
                                    pb[0][:, 0:512],
                                    _pan[:, di, 128 * _d4:128 * _d4 + 128],
                                    bufb[:, di * T + _scs.start:
                                         di * T + _scs.stop],
                                    start=(di == 0), stop=(di == 7))
                            return i
                        b.op('tensor', mmq, reads=(rp, rb), writes=(rpb[0],))
                        b.op('scalar', lambda e: e.copy(qraw, pb[0][:, 0:512]),
                             reads=(rpb[0],), writes=(rscr,))
                        def ropet(e, _scs=scs):
                            e.tensor_tensor(tmp1, qraw, CCv[:, _scs], ALU.mult)
                            return e.tensor_tensor(tmp2, qraw, SS[:, _scs],
                                                   ALU.mult)
                        b.op('vector', ropet, reads=(rscr, rc), writes=(rscr,))
                        b.op('tensor', lambda e: e.matmul(
                            pb[3][:, 0:512], self.v_p32, tmp2, start=True,
                            stop=True), reads=(rscr, r['cst']),
                            writes=(rpb[3],))
                        b.op('vector', lambda e: e.tensor_tensor(
                            tmp1, tmp1, pb[3][:, 0:512], ALU.add),
                            reads=(rpb[3],), writes=(rscr,))
                        b.op('vector', lambda e, _dst=dst, _dout=dout:
                             e.tensor_scalar(_dst[:, bass.ts(_dout, 512)],
                                             tmp1, 0.0, None, ALU.max),
                             reads=(rscr,), writes=(rc,))
            n_ci = 0 if self.dbg == 'proj1' else (
                1 if self.dbg in ('chunk1', 'vonly', 'ktok', 'g1', 'ga', 'gn', 'gu', 'gd', 'gm') else 4)
            for ci in range(n_ci):
                c = 4 * sc + ci
                ccs = slice(128 * ci, 128 * ci + 128)
                gcs = slice(128 * c, 128 * c + 128)
                for vb in range(2):
                    def mmv(e, _vb=vb, _gcs=gcs):
                        for di in range(8):
                            i = e.matmul(
                                pb[_vb][:, 0:512],
                                bufb[:, di * T + _gcs.start:di * T + _gcs.stop],
                                WVv[:, di, 512 * _vb:512 * _vb + 512],
                                start=(di == 0), stop=(di == 7))
                        return i
                    b.op('tensor', mmv, reads=(rb, rd), writes=(rpb[vb],))
                    def evv(e, _vb=vb):
                        return e.tensor_copy(
                            VA.rearrange('p (h c) -> p h c', c=65)
                            [:, 8 * _vb:8 * _vb + 8, 0:64],
                            pb[_vb][:, 0:512].rearrange(
                                'p (h c) -> p h c', c=64))
                    b.op('vector', evv, reads=(rpb[vb],), writes=(rc,))
                if self.dbg == 'vonly':
                    continue
                def ktr(e, _ccs=ccs):
                    for v8 in range(8):
                        i = e.transpose(pt0[:, 128 * v8:128 * v8 + 128],
                                        KTv[:, v8, _ccs], self.v_eye)
                    return i
                b.op('tensor', ktr, reads=(rc, r['cst']), writes=(rpt0,))
                b.op('vector', lambda e: e.tensor_copy(KTOK, pt0[:, :]),
                     reads=(rpt0,), writes=(rc,))
                if self.dbg == 'ktok':
                    continue
                def scast(e):
                    e.tensor_copy(SBFv[0:64, 0::2, :], Sfv[0:64, :, :])
                    return e.tensor_copy(SBFv[64:128, 1::2, :],
                                         Sfv[64:128, :, :])
                b.op('vector', scast, reads=(rscf,), writes=(rc,))
                for g in range(4):
                    if (self.dbg or '').startswith('g') and g > 0:
                        continue
                    heads = [[0, 2, 4, 6], [8, 10, 12, 14],
                             [1, 3, 5, 7], [9, 11, 13, 15]][g]
                    rb_ = 0 if g < 2 else 64
                    def mma(e, _hs=heads, _rb=rb_, _ccs=ccs):
                        for j, h in enumerate(_hs):
                            vt = h // 2
                            i = e.matmul(
                                pb[0][:, 128 * j:128 * j + 128],
                                KTv[_rb:_rb + 64, vt, _ccs],
                                QTv[_rb:_rb + 64, vt, _ccs],
                                start=(j == 0), stop=(j == 3))
                        return i
                    b.op('tensor', mma, reads=(rc,), writes=(rpb[0],))
                    if self.dbg == 'gm':
                        continue
                    b.op('vector', lambda e: e.tensor_tensor(
                        am_bf, pb[0][:, 0:512], self.v_tri4, ALU.mult),
                        reads=(rpb[0], r['cst']), writes=(rscr,))
                    if self.dbg == 'ga':
                        continue
                    def mmn(e, _hs=heads, _ccs=ccs):
                        for j, h in enumerate(_hs):
                            vt = h // 2
                            e.matmul(pb[1][0:65, 128 * j:128 * j + 128],
                                     VA[:, 65 * h:65 * h + 65],
                                     am_bf[:, 128 * j:128 * j + 128],
                                     start=(j == 0), stop=False)
                            i = e.matmul(
                                pb[1][0:65, 128 * j:128 * j + 128],
                                SBF[:, 65 * h:65 * h + 65],
                                QTv[:, vt, _ccs],
                                start=False, stop=(j == 3))
                        return i
                    b.op('tensor', mmn, reads=(rscr, rc), writes=(rpb[1],))
                    if self.dbg == 'gn':
                        continue
                    def mmu(e, _hs=heads):
                        for j, h in enumerate(_hs):
                            i = e.matmul(
                                pb[2][0:64, 65 * j:65 * j + 65],
                                KTOK[:, 64 * h:64 * h + 64],
                                VA[:, 65 * h:65 * h + 65],
                                start=(j == 0), stop=(j == 3))
                        return i
                    b.op('tensor', mmu, reads=(rc,), writes=(rpb[2],))
                    def supd(e, _hs=heads, _rb=rb_):
                        for j, h in enumerate(_hs):
                            vt = h // 2
                            i = e.tensor_tensor(
                                Sf[_rb:_rb + 64, 65 * vt:65 * vt + 65],
                                Sf[_rb:_rb + 64, 65 * vt:65 * vt + 65],
                                pb[2][0:64, 65 * j:65 * j + 65], ALU.add)
                        return i
                    b.op('vector', supd, reads=(rpb[2], rscf), writes=(rscf,))
                    if self.dbg == 'gu':
                        continue
                    def dden(e):
                        return e.tensor_scalar(den_sb, pb[1][64:65, 0:512],
                                               1e-6, None, ALU.add)
                    b.op('vector', dden, reads=(rpb[1],), writes=(rscf,))
                    b.op('vector', lambda e: e.reciprocal(den_sb, den_sb),
                         reads=(rscf,), writes=(rscf,))
                    b.op('tensor', lambda e: e.matmul(
                        pb[3][0:64, 0:512], self.v_onesrowf[0:1, 0:64], den_sb,
                        start=True, stop=True),
                        reads=(rscf, r['cstf']), writes=(rpb[3],))
                    if self.dbg == 'gd':
                        continue
                    b.op('scalar', lambda e: e.copy(dbc_bf[0:64, :],
                                                    pb[3][0:64, 0:512]),
                         reads=(rpb[3],), writes=(rscr,))
                    b.op('scalar', lambda e: e.copy(nums_bf[0:64, :],
                                                    pb[1][0:64, 0:512]),
                         reads=(rpb[1],), writes=(rscr,))
                    def oo(e, _hs=heads, _rb=rb_, _gcs=gcs):
                        for j, h in enumerate(_hs):
                            vt = h // 2
                            i = e.tensor_tensor(
                                bufb[_rb:_rb + 64,
                                     vt * T + _gcs.start:vt * T + _gcs.stop],
                                nums_bf[0:64, 128 * j:128 * j + 128],
                                dbc_bf[0:64, 128 * j:128 * j + 128], ALU.mult)
                        return i
                    b.op('vector', oo, reads=(rscr,), writes=(rb,))
        if self.dbg in ('chunk1', 'vonly', 'ktok', 'g1', 'ga', 'gn', 'gu', 'gd', 'gm'):
            self.dump_bf(bufb, rb)
            return
        if self.dbg in ('qt', 'proj1'):
            def cpq(e):
                for k in range(8):
                    i = e.tensor_copy(self.xt[:, k // 2, 1024 * (k % 2):
                                              1024 * (k % 2) + 512],
                                      bufc[:, bass.ts(k, 512)])
                return i
            b.op('vector', cpq, reads=(rc,), writes=(r['x'],))
            return
        if self.dbg == 'att':
            self.dump_bf(bufb, rb)
            return
        WO = bufd[:, 8192:16384]
        WOv = WO.rearrange('p (n o) -> p n o', o=1024)
        b.op('gpsimd', lambda e: e.dma_start(
            out=WOv, in_=self.hw['wo0'].rearrange('(n p) o -> p n o', p=128)),
            reads=(rd,), writes=(rd,), dma=True)
        xc, rxc = self.xc, r['xc']
        for c4 in range(4):
            cs = slice(512 * c4, 512 * c4 + 512)
            for dout in range(8):
                def mmo(e, _dout=dout, _cs=cs):
                    for di in range(8):
                        i = e.matmul(
                            pb[0][:, 0:512],
                            WOv[:, di, 128 * _dout:128 * _dout + 128],
                            bufb[:, di * T + _cs.start:di * T + _cs.stop],
                            start=(di == 0), stop=(di == 7))
                    return i
                b.op('tensor', mmo, reads=(rd, rb), writes=(rpb[0],))
                b.op('scalar', lambda e, _dout=dout: e.copy(
                    xc[:, bass.ts(_dout, 512)], pb[0][:, 0:512]),
                    reads=(rpb[0],), writes=(rxc,))
            self.ln_resid_chunk(c4, lambda dt_: xc[:, bass.ts(dt_, 512)], rxc)

    def _bd_compose(self):
        b, r = self.b, self.r
        bufb, bufc = self.bufb, self.bufc
        rb, rc = r['b'], r['c']
        X = bufc[0:64, 0:4096]
        X2 = bufc[0:64, 4096:8192]
        WD = bufc[0:64, 8192:12288]
        WO = bufc[0:64, 12288:16384]
        bds = [bufb[:, 0:4096], bufb[:, 4096:8192],
               bufb[:, 8192:12288], bufb[:, 12288:16384]]
        specs = [('bfw_up', range(0, 6), 0), ('bfw_up', range(6, 12), 1),
                 ('bfw_dn', range(0, 6), 2), ('bfw_dn', range(6, 12), 3)]
        for wname, stages, bi_ in specs:
            b.op('sync', lambda e: e.dma_start(out=X, in_=self.hw['x0c']),
                 writes=(rc,), dma=True)
            cur, nxt = X, X2
            for s in stages:
                bl = s if s < 6 else s - 6
                st = 2 ** bl
                b.op('gpsimd', lambda e, _w=wname, _s=s: e.dma_start(
                    out=WD, in_=self.hw[_w + '_bc'][_s, 0]),
                    writes=(rc,), dma=True)
                b.op('gpsimd', lambda e, _w=wname, _s=s: e.dma_start(
                    out=WO, in_=self.hw[_w + '_bc'][_s, 1]),
                    writes=(rc,), dma=True)
                b.op('vector', lambda e, _cur=cur, _nxt=nxt: e.tensor_tensor(
                    _nxt, _cur, WD, ALU.mult), reads=(rc,), writes=(rc,))
                b.op('vector', lambda e, _cur=cur: e.tensor_tensor(
                    WD, _cur, WO, ALU.mult), reads=(rc,), writes=(rc,))
                def stage2(e, _nxt=nxt, _st=st):
                    nv = _nxt.rearrange('p (n two s) -> p n two s', two=2,
                                        s=_st)
                    zv = WD.rearrange('p (n two s) -> p n two s', two=2, s=_st)
                    e.tensor_tensor(nv[:, :, 0, :], nv[:, :, 0, :],
                                    zv[:, :, 1, :], ALU.add)
                    return e.tensor_tensor(nv[:, :, 1, :], nv[:, :, 1, :],
                                           zv[:, :, 0, :], ALU.add)
                b.op('vector', stage2, reads=(rc,), writes=(rc,))
                cur, nxt = nxt, cur
            bd = bds[bi_]
            b.op('vector', lambda e, _bd=bd: e.memset(_bd, 0.0),
                 writes=(rb,))
            def asm(e, _cur=cur, _bd=bd):
                bv = _bd.rearrange('p (v c) -> p v c', c=128)
                xv = _cur.rearrange('q (v c) -> q v c', c=128)
                e.tensor_copy(bv[0:64, :, 0:64], xv[:, :, 0:64])
                return e.tensor_copy(bv[64:128, :, 64:128], xv[:, :, 64:128])
            b.op('vector', asm, reads=(rc,), writes=(rb,))
        b.op('sync', lambda e: e.dma_start(out=self.bdscr, in_=bufb[:, :]),
             reads=(rb,), writes=(r['bdscr'],), dma=True)

    def emit_l0_ffn(self, bi):
        self.merge_aliases()
        b, r = self.b, self.r
        xt, bufb, bufc, bufd = self.xt, self.bufb, self.bufc, self.bufd
        rb, rc, rd = r['b'], r['c'], r['d']
        pb, rpb = self.pb, self.rpb
        xc, rxc = self.xc, r['xc']
        if self.dbg in ('f8', 'f11', 'f12', 'f13', 'f14', 'f15', 'f16', 'f17', 'f18', 'f19', 'f20'):
            self._bd_done = True
        if not getattr(self, '_bias_done', False):
            self._bias_done = True
            for nm_, sl_ in (('b_exp', (0, 32)), ('up_bias_A', (32, 64)),
                             ('b_con', (64, 72)), ('down_bias', (72, 80))):
                b.op('sync', lambda e, _n=nm_, _s=sl_: e.dma_start(
                    out=self.bias_sb[:, _s[0]:_s[1]], in_=self.hw[_n]),
                    writes=(r['bias'],), dma=True)
            b.op('vector', lambda e: e.tensor_tensor(
                self.bias_sb[:, 80:88], self.bias_sb[:, 64:72],
                self.bias_sb[:, 72:80], ALU.add),
                reads=(r['bias'],), writes=(r['bias'],))
        if not getattr(self, '_bd_done', False):
            self._bd_compose()
            self._bd_done = True
        elif self.dbg != 'f20':
            b.op('sync', lambda e: e.dma_start(out=bufb[:, :], in_=self.bdscr),
                 reads=(r['bdscr'],), writes=(rb,), dma=True)
        bds = [bufb[:, 0:4096].rearrange('p (v c) -> p v c', c=128),
               bufb[:, 4096:8192].rearrange('p (v c) -> p v c', c=128),
               bufb[:, 8192:12288].rearrange('p (v c) -> p v c', c=128),
               bufb[:, 12288:16384].rearrange('p (v c) -> p v c', c=128)]
        H = bufc[:, :].rearrange('p (v t) -> p v t', t=512)
        XL = bufd[:, 8192:12288]
        panes = [bufd[:, 0:4096], bufd[:, 4096:8192]]
        rpan = self.rpan
        hscr_w = self.hscr.rearrange('(u p) t -> p u t', u=32, p=128)
        hscr_r = self.hscr.rearrange('(hi lo2 d2) t -> d2 hi lo2 t',
                                     hi=64, lo2=32, d2=2)
        fstop = None
        if (self.dbg or '').startswith('f'):
            fstop = int(self.dbg[1:])
        if fstop == 11:
            fstop = 10
        if fstop == 12:
            fstop = 0
        if fstop == 13:
            fstop = 0
        if fstop == 14:
            fstop = 0
        if fstop == 15:
            fstop = 0
        if fstop in (16, 17):
            pass
        if fstop in (18, 19):
            pass
        if fstop == 20:
            pass
        if fstop == 9:
            self.dump_bf(bufb, rb)
            return
        if fstop == 8:
            fstop = 0
        def fdump(buf=None):
            def cp(e, _b=buf if buf is not None else bufc):
                i = e.tensor_copy(
                    xt[:, :, :].rearrange('p a t -> p (a t)'), _b[:, :])
                return i
            b.op('vector', cp, reads=(rc, rd), writes=(r['x'],))
        for c4 in range(1 if fstop is not None else 4):
            cs = slice(512 * c4, 512 * c4 + 512)
            self.ln_pre_chunk(c4, lambda dt_: xt[:, dt_, :], r['x'],
                              lambda dt_: XL[:, bass.ts(dt_, 512)][:, 0:512],
                              rd)
            if fstop == 10:
                fdump()
                return
            for ph in range(8):
                pan = panes[ph % 2]
                rp0, rp1 = self.rps[ph % 2]
                for st in range(2):
                    b.op('gpsimd', lambda e, _ph=ph, _pan=pan, _s=st:
                         e.dma_start(
                             out=_pan.rearrange('p (n o) -> p n o', o=512)
                             [:, 4 * _s:4 * _s + 4, :],
                             in_=self.hw['wexp'][:, 512 * _ph:512 * _ph + 512]
                             .rearrange('(n p) o -> p n o', p=128)
                             [:, 4 * _s:4 * _s + 4, :]),
                         writes=(self.rps[ph % 2][st],), dma=True,
                         sem='Dg%d' % (2 * (ph % 2) + st))
                for d4 in range(0 if self.dbg == 'f13' else 4):
                    v = 4 * ph + d4
                    def mme(e, _pan=pan, _d4=d4):
                        for di in range(8):
                            i = e.matmul(
                                pb[_d4 % 2][:, 0:512],
                                _pan.rearrange('p (n o) -> p n o', o=512)
                                [:, di, 128 * _d4:128 * _d4 + 128],
                                XL.rearrange('p (n o) -> p n o', o=512)
                                [:, di, :],
                                start=(di == 0), stop=(di == 7))
                        return i
                    b.op('tensor', mme, reads=(rp0, rp1, rd),
                         writes=(rpb[d4 % 2],))
                    if self.dbg == 'f14':
                        continue
                    if self.dbg == 'f15':
                        b.op('vector', lambda e, _v=v, _d4=d4: e.tensor_copy(
                            H[:, _v, :], pb[_d4 % 2][:, 0:512]),
                            reads=(rpb[d4 % 2],), writes=(rc,))
                    elif self.dbg == 'f12':
                        b.op('scalar', lambda e, _v=v, _d4=d4: e.copy(
                            H[:, _v, :], pb[_d4 % 2][:, 0:512]),
                            reads=(rpb[d4 % 2],), writes=(rc,))
                    else:
                        b.op('scalar', lambda e, _v=v, _d4=d4: e.activation(
                            H[:, _v, :], pb[_d4 % 2][:, 0:512], AF.Identity,
                            bias=self.bias_sb[:, _v:_v + 1], scale=1.0),
                            reads=(rpb[d4 % 2], r['bias']), writes=(rc,))
            if fstop in (16, 17, 20):
                fdump(bufd)
                return
            if fstop == 0:
                fdump()
                return
            for step in range(4):
                bd = bds[step]
                for v in range(32):
                    def mmb(e, _bd=bd, _v=v):
                        return e.matmul(pb[_v % 2][:, 0:512], _bd[:, _v, :],
                                        H[:, _v, :], start=True, stop=True)
                    b.op('tensor', mmb, reads=(rb, rc), writes=(rpb[v % 2],))
                    if step == 1:
                        b.op('scalar', lambda e, _v=v: e.activation(
                            H[:, _v, :], pb[_v % 2][:, 0:512], AF.Gelu,
                            bias=self.bias_sb[:, 32 + _v:33 + _v], scale=1.0),
                            reads=(rpb[v % 2], r['bias']), writes=(rc,))
                    else:
                        b.op('scalar', lambda e, _v=v: e.copy(
                            H[:, _v, :], pb[_v % 2][:, 0:512]),
                            reads=(rpb[v % 2],), writes=(rc,))
                if fstop == 1 + 2 * step:
                    fdump()
                    return
                if step == 3:
                    break
                for u4 in range(4):
                    b.op('sync', lambda e, _u=u4: e.dma_start(
                        out=hscr_w[:, 8 * _u:8 * _u + 8, :],
                        in_=H[:, 8 * _u:8 * _u + 8, :]),
                        reads=(rc,), writes=(r['hscr'],), dma=True)
                for d2 in range(2):
                    for a4 in range(4):
                        def shr(e, _d=d2, _a=a4):
                            return e.dma_start(
                                out=H[64 * _d + 16 * _a:64 * _d + 16 * _a + 16,
                                      :, :],
                                in_=hscr_r[_d, 16 * _a:16 * _a + 16, :, :])
                        b.op('gpsimd', shr, reads=(r['hscr'],), writes=(rc,),
                             dma=True)
                if fstop == 2 + 2 * step:
                    fdump()
                    return
            for dout in range(8):
                for vp in range(8):
                    pan = panes[vp % 2]
                    rp0, rp1 = self.rps[vp % 2]
                    for st in range(2):
                        b.op('gpsimd', lambda e, _d=dout, _vp=vp, _pan=pan,
                             _s=st: e.dma_start(
                                 out=_pan[:, 0:512].rearrange(
                                     'p (n o) -> p n o', o=128)
                                 [:, 2 * _s:2 * _s + 2, :],
                                 in_=self.hw['wcon'][
                                     512 * _vp:512 * _vp + 512,
                                     128 * _d:128 * _d + 128].rearrange(
                                         '(n p) o -> p n o', p=128)
                                 [:, 2 * _s:2 * _s + 2, :]),
                             writes=(self.rps[vp % 2][st],), dma=True,
                             sem='Dg%d' % (2 * (vp % 2) + st))
                    rp = None
                    def mmc(e, _vp=vp, _pan=pan):
                        for k in range(4):
                            i = e.matmul(
                                pb[2][:, 0:512],
                                _pan[:, 0:512].rearrange(
                                    'p (n o) -> p n o', o=128)[:, k, :],
                                H[:, 4 * _vp + k, :],
                                start=(_vp == 0 and k == 0),
                                stop=(_vp == 7 and k == 3))
                        return i
                    b.op('tensor', mmc, reads=(rp0, rp1, rc),
                         writes=(rpb[2],))
                b.op('scalar', lambda e, _d=dout: e.activation(
                    xc[:, bass.ts(_d, 512)], pb[2][:, 0:512], AF.Identity,
                    bias=self.bias_sb[:, 80 + _d:80 + _d + 1],
                    scale=1.0),
                    reads=(rpb[2], r['bias']), writes=(rxc,))
            self.ln_resid_chunk(c4, lambda dt_: xc[:, bass.ts(dt_, 512)], rxc)


    def emit_l1_attn(self, bi):
        self.merge_aliases()
        b, r = self.b, self.r
        xt, bufb, bufc, bufd = self.xt, self.bufb, self.bufc, self.bufd
        scr, scf = self.scr, self.scf
        rb, rc, rd = r['b'], r['c'], r['d']
        rscr, rscf = r['scr'], r['scf']
        pb, rpb = self.pb, self.rpb
        xc, rxc = self.xc, r['xc']
        rpan = self.rpan
        VA = bufd[:, 0:4160]
        VAg = VA.rearrange('p (k g c) -> p k g c', g=4, c=65)
        QT = bufd[:, 4160:8256]
        QTv = QT.rearrange('p (n o) -> p n o', o=512)
        KRv = bufc[:, :].rearrange('p (n o) -> p n o', o=2048)
        qraw = scr[:, 1536:2048]
        tmp1 = scr[:, 2048:2560]
        tmp2 = scr[:, 2560:3072]
        tbl_c = scr[:, 3072:3584]
        tbl_s = scr[:, 3584:4096]
        e1 = scr[:, 0:512]
        e2 = scr[:, 512:1024]
        yslot = scr[:, 1024:1536]
        rows = scf

        self.ln_full(lambda dt_: xt[:, dt_, :], r['x'], bufb, rb)
        if not getattr(self, '_l1_const_done', False):
            self._l1_const_done = True
            b.op('sync', lambda e: e.dma_start(out=scf[0:1, 2240:2256],
                                              in_=self.hw['lam']),
                 writes=(rscf,), dma=True, sem='Dk')
            b.op('sync', lambda e: e.dma_start(out=scf[0:64, 2256:2257],
                                              in_=self.hw['rmsw']),
                 writes=(rscf,), dma=True, sem='Dk')
            b.op('scalar', lambda e: e.activation(
                scf[0:1, 2240:2256], scf[0:1, 2240:2256], AF.Sigmoid),
                reads=(rscf,), writes=(rscf,))
        pane0 = bufd[:, 8256:10304]
        pane1 = bufd[:, 10304:12352]
        p0v = pane0.rearrange('p (n o) -> p n o', o=256)
        p1v = pane1.rearrange('p (n o) -> p n o', o=256)
        b.op('gpsimd', lambda e: e.dma_start(
            out=p0v, in_=self.hw['wk1'].rearrange('(n p) o -> p n o', p=128)),
            writes=(rpan[0],), dma=True)
        b.op('gpsimd', lambda e: e.dma_start(
            out=p1v, in_=self.hw['wv1'].rearrange('(n p) o -> p n o', p=128)),
            writes=(rpan[1],), dma=True)
        for kt in range(2):
            for c4 in range(4):
                cs = slice(512 * c4, 512 * c4 + 512)
                def mmk(e, _kt=kt, _cs=cs):
                    for di in range(8):
                        i = e.matmul(pb[0][:, 0:512],
                                     p0v[:, di, 128 * _kt:128 * _kt + 128],
                                     bufb[:, di * T + _cs.start:
                                          di * T + _cs.stop],
                                     start=(di == 0), stop=(di == 7))
                    return i
                b.op('tensor', mmk, reads=(rpan[0], rb), writes=(rpb[0],))
                b.op('scalar', lambda e: e.copy(qraw, pb[0][:, 0:512]),
                     reads=(rpb[0],), writes=(rscr,))
                b.op('sync', lambda e, _cs=cs: e.dma_start(
                    out=tbl_c, in_=self.hw['cc1k'][:, _cs]),
                    writes=(rscr,), dma=True, sem='Dk')
                b.op('sync', lambda e, _cs=cs: e.dma_start(
                    out=tbl_s, in_=self.hw['ss1k'][:, _cs]),
                    writes=(rscr,), dma=True, sem='Dk')
                def ropet(e):
                    e.tensor_tensor(tmp1, qraw, tbl_c, ALU.mult)
                    return e.tensor_tensor(tmp2, qraw, tbl_s, ALU.mult)
                b.op('vector', ropet, reads=(rscr,), writes=(rscr,))
                b.op('tensor', lambda e: e.matmul(
                    pb[3][:, 0:512], self.v_p16, tmp2, start=True, stop=True),
                    reads=(rscr, r['cst']), writes=(rpb[3],))
                b.op('vector', lambda e: e.tensor_tensor(
                    tmp1, tmp1, pb[3][:, 0:512], ALU.add),
                    reads=(rpb[3], rscr), writes=(rscr,))
                def repl(e, _kt=kt, _cs=cs):
                    for j in range(4):
                        tj = 4 * _kt + j
                        for half in range(2):
                            hq = 2 * tj + half
                            sb_ = ((hq // 4) % 2) * 64
                            i = e.tensor_copy(
                                KRv[64 * half:64 * half + 64, tj, _cs],
                                tmp1[sb_:sb_ + 64, :])
                    return i
                b.op('vector', repl, reads=(rscr,), writes=(rc,))
        first = True
        for tj in (0, 4):
            for c4 in range(4):
                b.op('scalar', lambda e, _tj=tj, _c=c4: e.activation(
                    tmp1, KRv[:, _tj, 512 * _c:512 * _c + 512], AF.Square),
                    reads=(rc,), writes=(rscr,))
                b.op('tensor', lambda e: e.matmul(
                    pb[1][0:1, 0:512], self.v_ones_bf[:, 0:1], tmp1,
                    start=True, stop=True),
                    reads=(rscr, r['cst']), writes=(rpb[1],))
                if first:
                    b.op('vector', lambda e: e.tensor_copy(
                        rows[0:1, 1088:1600], pb[1][0:1, 0:512]),
                        reads=(rpb[1],), writes=(rscf,))
                    first = False
                else:
                    b.op('vector', lambda e: e.tensor_tensor(
                        rows[0:1, 1088:1600], rows[0:1, 1088:1600],
                        pb[1][0:1, 0:512], ALU.max),
                        reads=(rpb[1], rscf), writes=(rscf,))
        b.op('vector', lambda e: e.reduce_max(
            rows[0:1, 1057:1058], rows[0:1, 1088:1600], axis=AX),
            reads=(rscf,), writes=(rscf,))
        b.op('vector', lambda e: e.memset(VA, 0.0), writes=(rd,))
        b.op('vector', lambda e: e.memset(VAg[:, :, :, 64:65], 1.0),
             writes=(rd,))
        for tt in range(16):
            def mmv(e, _tt=tt):
                for di in range(8):
                    i = e.matmul(pb[0][:, 0:256],
                                 bufb[:, di * T + 128 * _tt:
                                      di * T + 128 * _tt + 128],
                                 p1v[:, di, :],
                                 start=(di == 0), stop=(di == 7))
                return i
            b.op('tensor', mmv, reads=(rb, rpan[1]), writes=(rpb[0],))
            b.op('scalar', lambda e, _tt=tt: e.copy(
                VAg[:, _tt, :, 0:64],
                pb[0][:, 0:256].rearrange('p (g c) -> p g c', c=64)),
                reads=(rpb[0],), writes=(rd,))
        a_bal = float(np.sqrt(32.0))
        for qg in range(4):
            self.merge_aliases()
            qs = slice(512 * qg, 512 * qg + 512)
            for ph in range(4):
                pane, rp = ((pane0, rpan[0]) if ph % 2 == 0
                            else (pane1, rpan[1]))
                pv = pane.rearrange('p (n o) -> p n o', o=256)
                b.op('gpsimd', lambda e, _ph=ph, _pv=pv: e.dma_start(
                    out=_pv, in_=self.hw['wq1'][:, 256 * _ph:256 * _ph + 256]
                    .rearrange('(n p) o -> p n o', p=128)),
                    writes=(rp,), dma=True)
                for d2 in range(2):
                    dt_ = 2 * ph + d2
                    def mmq(e, _pv=pv, _d2=d2, _qs=qs):
                        for di in range(8):
                            i = e.matmul(pb[0][:, 0:512],
                                         _pv[:, di, 128 * _d2:128 * _d2 + 128],
                                         bufb[:, di * T + _qs.start:
                                              di * T + _qs.stop],
                                         start=(di == 0), stop=(di == 7))
                        return i
                    b.op('tensor', mmq, reads=(rp, rb), writes=(rpb[0],))
                    b.op('scalar', lambda e: e.copy(qraw, pb[0][:, 0:512]),
                         reads=(rpb[0],), writes=(rscr,))
                    b.op('sync', lambda e, _qs=qs: e.dma_start(
                        out=tbl_c, in_=self.hw['cc1q'][:, _qs]),
                        writes=(rscr,), dma=True, sem='Dk')
                    b.op('sync', lambda e, _qs=qs: e.dma_start(
                        out=tbl_s, in_=self.hw['ss1q'][:, _qs]),
                        writes=(rscr,), dma=True, sem='Dk')
                    def ropet(e):
                        e.tensor_tensor(tmp1, qraw, tbl_c, ALU.mult)
                        return e.tensor_tensor(tmp2, qraw, tbl_s, ALU.mult)
                    b.op('vector', ropet, reads=(rscr,), writes=(rscr,))
                    b.op('tensor', lambda e: e.matmul(
                        pb[3][:, 0:512], self.v_p16, tmp2, start=True,
                        stop=True), reads=(rscr, r['cst']), writes=(rpb[3],))
                    b.op('vector', lambda e, _dt=dt_: e.tensor_tensor(
                        QTv[:, _dt, :], tmp1, pb[3][:, 0:512], ALU.add),
                        reads=(rpb[3], rscr), writes=(rd,))
            for t in range(8):
                b.op('scalar', lambda e, _t=t: e.activation(
                    tmp1, QTv[:, _t, :], AF.Square),
                    reads=(rd,), writes=(rscr,))
                b.op('tensor', lambda e: e.matmul(
                    pb[1][0:1, 0:512], self.v_ones_bf[:, 0:1], tmp1,
                    start=True, stop=True),
                    reads=(rscr, r['cst']), writes=(rpb[1],))
                b.op('vector', lambda e, _t=t: e.reduce_max(
                    rows[0:1, _t:_t + 1], pb[1][0:1, 0:512], axis=AX),
                    reads=(rpb[1],), writes=(rscf,))
            def mrow(e):
                e.tensor_scalar(rows[0:1, 8:16], rows[0:1, 0:8],
                                a_bal * 0.5, None, ALU.mult)
                return e.tensor_scalar(rows[0:1, 2112:2113],
                                       rows[0:1, 1057:1058],
                                       0.5 / a_bal, None, ALU.mult)
            b.op('vector', mrow, reads=(rscf,), writes=(rscf,))
            b.op('vector', lambda e: e.tensor_scalar(
                rows[0:1, 8:16], rows[0:1, 8:16], rows[0:1, 2112:2113], 1.0,
                ALU.add, ALU.add), reads=(rscf,), writes=(rscf,))
            b.op('vector', lambda e: e.tensor_scalar(
                rows[0:1, 8:16], rows[0:1, 8:16], -1.0, None, ALU.mult),
                reads=(rscf,), writes=(rscf,))
            b.op('tensor', lambda e: e.matmul(
                pb[1][:, 0:8], self.v_onesrowf, rows[0:1, 8:16],
                start=True, stop=True),
                reads=(rscf, r['cstf']), writes=(rpb[1],))
            b.op('vector', lambda e: e.tensor_copy(
                scf[:, 2176:2184], pb[1][:, 0:8]),
                reads=(rpb[1],), writes=(rscf,))
            nkc = 4 * qg + 4
            self.merge_aliases()
            ebufs = [e1, e2, tbl_c, tbl_s]
            for h in range(16):
                tj = h // 2
                kvh = h // 4
                U = [(mat, kc) for mat in range(2) for kc in range(nkc)]
                for idx in range(len(U) + 2):
                    if idx < len(U):
                        mat, kc = U[idx]
                        rbm = (h % 2) * 64 + 32 * mat
                        spb, rspb = pb[idx % 4], rpb[idx % 4]
                        def mms(e, _rbm=rbm, _tj=tj, _kc=kc, _sp=spb):
                            return e.matmul(
                                _sp[:, 0:512],
                                KRv[_rbm:_rbm + 32, _tj,
                                    128 * _kc:128 * _kc + 128],
                                QTv[_rbm:_rbm + 32, _tj, :],
                                start=True, stop=True,
                                tile_position=(_rbm, 0))
                        b.op('tensor', mms, reads=(rc, rd), writes=(rspb,))
                    if idx >= 2:
                        jdx = idx - 2
                        mat, kc = U[jdx]
                        ebuf, reb = ebufs[jdx % 4], self.re[jdx % 4]
                        spb, rspb = pb[jdx % 4], rpb[jdx % 4]
                        ypb = pb[4 + mat]
                        rypb = self.rpb[4 + mat]
                        b.op('scalar', lambda e, _tj=tj, _eb=ebuf, _sp=spb:
                             e.activation(_eb, _sp[:, 0:512], AF.Exp,
                                          bias=scf[:, 2176 + _tj:2177 + _tj],
                                          scale=1.0),
                             reads=(rspb, rscf), writes=(reb,))
                        j = kc - 4 * qg
                        if j >= 0:
                            b.op('vector', lambda e, _j=j, _eb=ebuf:
                                 e.tensor_tensor(
                                     _eb, _eb,
                                     self.v_stair[:, 384 - 128 * _j:
                                                  896 - 128 * _j], ALU.mult),
                                 reads=(r['cst'],), writes=(reb,))
                        def mmav(e, _kvh=kvh, _kc=kc, _eb=ebuf,
                                 _f=(kc == 0), _l=(kc == nkc - 1), _yp=ypb):
                            return e.matmul(
                                _yp[0:65, 0:512],
                                VA[:, 260 * _kc + 65 * _kvh:
                                   260 * _kc + 65 * _kvh + 65],
                                _eb, start=_f, stop=_l)
                        b.op('tensor', mmav, reads=(rd, reb),
                             writes=(rypb,))
                for mat in range(2):
                    ypb = pb[4 + mat]
                    rypb = self.rpb[4 + mat]
                    urow = rows[0:1, 16 + 512 * mat:528 + 512 * mat]
                    b.op('vector', lambda e, _yp=ypb, _u=urow: e.tensor_copy(
                        _u, _yp[64:65, 0:512]), reads=(rypb,), writes=(rscf,))
                    b.op('vector', lambda e, _u=urow: e.reciprocal(_u, _u),
                         reads=(rscf,), writes=(rscf,))
                    if mat == 1:
                        b.op('vector', lambda e, _u=urow, _h=h:
                             e.tensor_scalar(_u, _u,
                                             scf[0:1, 2240 + _h:2241 + _h],
                                             None, ALU.mult),
                             reads=(rscf,), writes=(rscf,))
                for mat in range(2):
                    urow = rows[0:1, 16 + 512 * mat:528 + 512 * mat]
                    b.op('tensor', lambda e, _u=urow: e.matmul(
                        pb[2][0:64, 0:512], self.v_onesrowf[0:1, 0:64], _u,
                        start=True, stop=True),
                        reads=(rscf, r['cstf']), writes=(rpb[2],))
                    dst = tmp1 if mat == 0 else tmp2
                    b.op('scalar', lambda e, _d=dst: e.copy(
                        _d[0:64, :], pb[2][0:64, 0:512]),
                        reads=(rpb[2],), writes=(rscr,))
                b.op('vector', lambda e: e.tensor_tensor(
                    yslot[0:64, :], pb[4][0:64, 0:512], tmp1[0:64, :],
                    ALU.mult), reads=(self.rpb[4], rscr), writes=(rscr,))
                b.op('vector', lambda e: e.tensor_tensor(
                    qraw[0:64, :], pb[5][0:64, 0:512], tmp2[0:64, :],
                    ALU.mult), reads=(self.rpb[5], rscr), writes=(rscr,))
                b.op('vector', lambda e: e.tensor_tensor(
                    yslot[0:64, :], yslot[0:64, :], qraw[0:64, :],
                    ALU.subtract), reads=(rscr,), writes=(rscr,))
                b.op('scalar', lambda e: e.activation(
                    tmp1[0:64, :], yslot[0:64, :], AF.Square),
                    reads=(rscr,), writes=(rscr,))
                b.op('tensor', lambda e: e.matmul(
                    pb[2][0:1, 0:512], self.v_ones_bf[0:64, 0:1],
                    tmp1[0:64, :], start=True, stop=True),
                    reads=(rscr, r['cst']), writes=(rpb[2],))
                ssrow = rows[0:1, 1600:2112]
                b.op('scalar', lambda e, _s=ssrow: e.activation(
                    _s, pb[2][0:1, 0:512], AF.Ln,
                    bias=self.v_colf[0:1, 1:2], scale=1.0 / 64.0),
                    reads=(rpb[2], r['cstf']), writes=(rscf,))
                b.op('scalar', lambda e, _s=ssrow: e.activation(
                    _s, _s, AF.Exp, scale=-0.5),
                    reads=(rscf,), writes=(rscf,))
                b.op('tensor', lambda e, _s=ssrow: e.matmul(
                    pb[2][0:64, 0:512], self.v_onesrowf[0:1, 0:64], _s,
                    start=True, stop=True),
                    reads=(rscf, r['cstf']), writes=(rpb[2],))
                b.op('vector', lambda e: e.scalar_tensor_tensor(
                    yslot[0:64, :], yslot[0:64, :], scf[0:64, 2256:2257],
                    pb[2][0:64, 0:512], ALU.mult, ALU.mult),
                    reads=(rscr, rscf, rpb[2]), writes=(rscr,))
                b.op('gpsimd', lambda e, _h=h, _qs=qs: e.dma_start(
                    out=self.yscr[64 * _h:64 * _h + 64, _qs],
                    in_=yslot[0:64, :]),
                    reads=(rscr,), writes=(r['yscr'],), dma=True, sem='Dy')
        self.merge_aliases()
        YT = bufc[:, 0:4096].rearrange('p (n o) -> p n o', o=512)
        for c4 in range(4):
            cs = slice(512 * c4, 512 * c4 + 512)
            b.op('sync', lambda e, _cs=cs: e.dma_start(
                out=YT, in_=self.yscr[:, _cs].rearrange(
                    '(n p) o -> p n o', p=128)),
                reads=(r['yscr'], rc), writes=(rc,), dma=True, sem='Dy2')
            for ph in range(2):
                pane, rp = ((bufd[:, 0:4096], rpan[0]) if ph == 0
                            else (bufd[:, 4096:8192], rpan[1]))
                pv = pane.rearrange('p (n o) -> p n o', o=512)
                b.op('gpsimd', lambda e, _ph=ph, _pv=pv: e.dma_start(
                    out=_pv, in_=self.hw['wout1'][:, 512 * _ph:512 * _ph + 512]
                    .rearrange('(n p) o -> p n o', p=128)),
                    writes=(rp,), dma=True)
                for d4 in range(4):
                    dout = 4 * ph + d4
                    def mmo(e, _pv=pv, _d4=d4):
                        for di in range(8):
                            i = e.matmul(pb[_d4 % 2][:, 0:512],
                                         _pv[:, di, 128 * _d4:128 * _d4 + 128],
                                         YT[:, di, :],
                                         start=(di == 0), stop=(di == 7))
                        return i
                    b.op('tensor', mmo, reads=(rp, rc), writes=(rpb[d4 % 2],))
                    b.op('scalar', lambda e, _d=dout, _d4=d4: e.copy(
                        xc[:, bass.ts(_d, 512)], pb[_d4 % 2][:, 0:512]),
                        reads=(rpb[d4 % 2],), writes=(rxc,))
            self.ln_resid_chunk(c4, lambda dt_: xc[:, bass.ts(dt_, 512)], rxc)


    def emit_l1_moe(self, bi):
        self.merge_aliases()
        b, r = self.b, self.r
        xt, bufb, bufc = self.xt, self.bufb, self.bufc
        scr, scf = self.scr, self.scf
        rb, rc, rd = r['b'], r['c'], r['d']
        rscr, rscf = r['scr'], r['scf']
        pb, rpb = self.pb, self.rpb
        xc, rxc = self.xc, r['xc']
        racc = self.bufdf
        raccv = racc[:, :].rearrange('p (n o) -> p n o', o=512)
        slots = [bufc[:, 4096 * i:4096 * i + 4096] for i in range(4)]
        rsl = self.rsl
        gsil = xc[:, 0:2048]
        gsv = gsil.rearrange('p (n o) -> p n o', o=512)
        cbc = xc[:, 2048:2560]
        P0 = scr[:, 0:2048]
        P1 = scr[:, 2048:4096]
        Pv = [P0.rearrange('p (n o) -> p n o', o=512),
              P1.rearrange('p (n o) -> p n o', o=512)]
        rP = self.rP
        comb = scf[0:16, 0:2048]
        crow = scf[0:1, 1088:1600]
        wgv = self.cst[:, 2112:2240].rearrange('p (n o) -> p n o', o=16)

        self.ln_full(lambda dt_: xt[:, dt_, :], r['x'], bufb, rb)
        if not getattr(self, '_moe_const_done', False):
            self._moe_const_done = True
            b.op('gpsimd', lambda e: e.dma_start(
                out=wgv, in_=self.hw['wgate'].rearrange(
                    '(n p) o -> p n o', p=128)),
                writes=(r['cst'],), dma=True, sem='Dr')
        s_ = scf[:, 2064:2080]
        sel = scf[:, 2080:2096]
        work = scf[:, 2096:2112]
        t_ = scf[:, 2112:2128]
        mx = scf[:, 2128:2129]
        for tt in range(16):
            def mmg(e, _tt=tt):
                for di in range(8):
                    i = e.matmul(pb[2][:, 0:16],
                                 bufb[:, di * T + 128 * _tt:
                                      di * T + 128 * _tt + 128],
                                 wgv[:, di, :], start=(di == 0),
                                 stop=(di == 7))
                return i
            b.op('tensor', mmg, reads=(rb, r['cst']), writes=(rpb[2],))
            b.op('scalar', lambda e: e.activation(s_, pb[2][:, 0:16],
                                                  AF.Sigmoid),
                 reads=(rpb[2],), writes=(rscf,))
            def grp(e):
                sA, sB = s_[:, 0:16:4], s_[:, 1:16:4]
                sC, sD = s_[:, 2:16:4], s_[:, 3:16:4]
                e.tensor_tensor(work[:, 0:4], sA, sB, ALU.max)
                e.tensor_tensor(work[:, 4:8], sA, sB, ALU.min)
                e.tensor_tensor(work[:, 8:12], sC, sD, ALU.max)
                return e.tensor_tensor(work[:, 12:16], sC, sD, ALU.min)
            b.op('vector', grp, reads=(rscf,), writes=(rscf,))
            def grp2(e):
                e.tensor_tensor(t_[:, 0:4], work[:, 4:8], work[:, 8:12],
                                ALU.max)
                return e.tensor_tensor(t_[:, 4:8], work[:, 12:16],
                                       work[:, 0:4], ALU.max)
            b.op('vector', grp2, reads=(rscf,), writes=(rscf,))
            def grp3(e):
                e.tensor_tensor(t_[:, 0:4], t_[:, 0:4], work[:, 0:4], ALU.add)
                return e.tensor_tensor(t_[:, 4:8], t_[:, 4:8], work[:, 8:12],
                                       ALU.add)
            b.op('vector', grp3, reads=(rscf,), writes=(rscf,))
            b.op('vector', lambda e: e.tensor_tensor(
                work[:, 0:4], t_[:, 0:4], t_[:, 4:8], ALU.max),
                reads=(rscf,), writes=(rscf,))
            def sec(e):
                e.tensor_tensor(t_[:, 0:1], work[:, 0:1], work[:, 1:2],
                                ALU.max)
                e.tensor_tensor(t_[:, 1:2], work[:, 0:1], work[:, 1:2],
                                ALU.min)
                e.tensor_tensor(t_[:, 2:3], work[:, 2:3], work[:, 3:4],
                                ALU.max)
                return e.tensor_tensor(t_[:, 3:4], work[:, 2:3], work[:, 3:4],
                                       ALU.min)
            b.op('vector', sec, reads=(rscf,), writes=(rscf,))
            def sec2(e):
                e.tensor_tensor(t_[:, 4:5], t_[:, 0:1], t_[:, 2:3], ALU.min)
                return e.tensor_tensor(t_[:, 5:6], t_[:, 1:2], t_[:, 3:4],
                                       ALU.max)
            b.op('vector', sec2, reads=(rscf,), writes=(rscf,))
            b.op('vector', lambda e: e.tensor_tensor(
                t_[:, 6:7], t_[:, 4:5], t_[:, 5:6], ALU.max),
                reads=(rscf,), writes=(rscf,))
            b.op('vector', lambda e: e.tensor_scalar(
                work[:, 0:4], work[:, 0:4], t_[:, 6:7], None, ALU.is_ge),
                reads=(rscf,), writes=(rscf,))
            def emask2(e):
                for k in range(4):
                    i = e.tensor_copy(t_[:, k:16:4], work[:, 0:4])
                return i
            b.op('vector', emask2, reads=(rscf,), writes=(rscf,))
            b.op('vector', lambda e: e.scalar_tensor_tensor(
                work[:, 0:16], t_[:, 0:16], 10.0, s_, ALU.mult, ALU.add),
                reads=(rscf,), writes=(rscf,))
            b.op('vector', lambda e: e.tensor_scalar(
                work[:, 0:16], work[:, 0:16], -10.0, None, ALU.add),
                reads=(rscf,), writes=(rscf,))
            b.op('vector', lambda e: e.memset(sel[:, :], 0.0),
                 writes=(rscf,))
            for rnd in range(4):
                b.op('vector', lambda e: e.reduce_max(mx, work[:, 0:16],
                                                      axis=AX),
                     reads=(rscf,), writes=(rscf,))
                b.op('vector', lambda e: e.tensor_scalar(
                    t_[:, 0:16], work[:, 0:16], mx, None, ALU.is_ge),
                    reads=(rscf,), writes=(rscf,))
                b.op('vector', lambda e: e.tensor_tensor(
                    sel[:, :], sel[:, :], t_[:, 0:16], ALU.add),
                    reads=(rscf,), writes=(rscf,))
                b.op('vector', lambda e: e.scalar_tensor_tensor(
                    work[:, 0:16], t_[:, 0:16], -1e9, work[:, 0:16],
                    ALU.mult, ALU.add), reads=(rscf,), writes=(rscf,))
            b.op('vector', lambda e: e.tensor_tensor(
                t_[:, 0:16], sel[:, :], s_, ALU.mult),
                reads=(rscf,), writes=(rscf,))
            b.op('vector', lambda e: e.reduce_sum(mx, t_[:, 0:16], axis=AX),
                 reads=(rscf,), writes=(rscf,))
            b.op('vector', lambda e: e.tensor_scalar(
                mx, mx, 1e-6, None, ALU.add), reads=(rscf,), writes=(rscf,))
            b.op('vector', lambda e: e.reciprocal(mx, mx),
                 reads=(rscf,), writes=(rscf,))
            b.op('vector', lambda e: e.tensor_scalar(
                t_[:, 0:16], t_[:, 0:16], mx, None, ALU.mult),
                reads=(rscf,), writes=(rscf,))
            if self.dbg == 'm0' and tt == 0:
                def dmp(e):
                    e.tensor_copy(xt[:, 0, 0:16], s_)
                    e.tensor_copy(xt[:, 0, 16:32], sel[:, :])
                    e.tensor_copy(xt[:, 0, 32:48], t_[:, 0:16])
                    i = e.tensor_copy(xt[:, 0, 48:64], work[:, 0:16])
                    return i
                b.op('vector', dmp, reads=(rscf,), writes=(r['x'],))
                return
            b.op('tensor', lambda e: e.transpose(
                pb[3][0:16, 0:128], t_[:, 0:16], self.v_eyef),
                reads=(rscf, r['cstf']), writes=(rpb[3],))
            b.op('vector', lambda e, _tt=tt: e.tensor_copy(
                comb[:, 128 * _tt:128 * _tt + 128], pb[3][0:16, 0:128]),
                reads=(rpb[3],), writes=(rscf,))
        if self.dbg == 'm1':
            b.op('vector', lambda e: e.tensor_copy(
                xt[0:16, 0, :], comb), reads=(rscf,), writes=(r['x'],))
            return
        b.op('sync', lambda e: e.dma_start(out=self.cscr, in_=comb),
             reads=(rscf,), writes=(r['cscr'],), dma=True, sem='Dr2')
        for c4 in range(4):
            self.merge_aliases()
            cs = slice(512 * c4, 512 * c4 + 512)
            first_acc = [True] * 8
            for ei in range(18):
                shared = ei >= 16
                half = ei - 16
                sl_w1 = slots[0] if (ei % 2 == 0) else slots[3]
                rs_w1 = rsl[0] if (ei % 2 == 0) else rsl[3]
                sl_w3, rs_w3 = slots[1], rsl[1]
                sl_w2 = slots[2] if (ei % 2 == 0) else slots[1]
                rs_w2 = rsl[2] if (ei % 2 == 0) else rsl[1]
                w1v = sl_w1.rearrange('p (n o) -> p n o', o=512)
                w3v = sl_w3.rearrange('p (n o) -> p n o', o=512)
                def _ld(e, _dst, _src):
                    return e.dma_start(out=_dst, in_=_src)
                if shared:
                    src1 = self.hw['w1s'][:, 512 * half:512 * half + 512]
                    src3 = self.hw['w3s'][:, 512 * half:512 * half + 512]
                    src2 = self.hw['w2s'][512 * half:512 * half + 512, :]
                else:
                    src1 = self.hw['w1e'][ei]
                    src3 = self.hw['w3e'][ei]
                    src2 = self.hw['w2e'][ei]
                b.op('gpsimd', lambda e, _s=sl_w1, _w=src1: e.dma_start(
                    out=_s.rearrange('p (n o) -> p n o', o=512),
                    in_=_w.rearrange('(n p) o -> p n o', p=128)),
                    writes=(rs_w1,), dma=True, sem='Dp0')
                b.op('gpsimd', lambda e, _w=src3: e.dma_start(
                    out=w3v, in_=_w.rearrange('(n p) o -> p n o', p=128)),
                    writes=(rs_w3,), dma=True, sem='Dp1')
                if ei % 2 == 0:
                    b.op('gpsimd', lambda e, _s=sl_w2, _w=src2: e.dma_start(
                        out=_s.rearrange('p (n o) -> p n o', o=1024),
                        in_=_w.rearrange('(n p) o -> p n o', p=128)),
                        writes=(rs_w2,), dma=True, sem='Dp2')
                if not shared:
                    b.op('gpsimd', lambda e, _e=ei, _cs=cs: e.dma_start(
                        out=crow, in_=self.cscr[_e:_e + 1, _cs]),
                        reads=(r['cscr'],), writes=(rscf,), dma=True,
                        sem='Dr')
                    b.op('tensor', lambda e: e.matmul(
                        pb[3][:, 0:512], self.v_onesrowf, crow,
                        start=True, stop=True),
                        reads=(rscf, r['cstf']), writes=(rpb[3],))
                    b.op('vector', lambda e: e.tensor_copy(
                        cbc, pb[3][:, 0:512]), reads=(rpb[3],),
                        writes=(rxc,))
                for f4 in range(4):
                    def mmG(e, _f=f4, _cs=cs, _w=w1v):
                        for di in range(8):
                            i = e.matmul(pb[0][:, 0:512],
                                         _w[:, di, 128 * _f:128 * _f + 128],
                                         bufb[:, di * T + _cs.start:
                                              di * T + _cs.stop],
                                         start=(di == 0), stop=(di == 7))
                        return i
                    b.op('tensor', mmG, reads=(rs_w1, rb), writes=(rpb[0],))
                    b.op('scalar', lambda e, _f=f4: e.activation(
                        gsv[:, _f, :], pb[0][:, 0:512], AF.Silu),
                        reads=(rpb[0],), writes=(rxc,))
                    def mmU(e, _f=f4, _cs=cs, _w=w3v):
                        for di in range(8):
                            i = e.matmul(pb[1][:, 0:512],
                                         _w[:, di, 128 * _f:128 * _f + 128],
                                         bufb[:, di * T + _cs.start:
                                              di * T + _cs.stop],
                                         start=(di == 0), stop=(di == 7))
                        return i
                    b.op('tensor', mmU, reads=(rs_w3, rb), writes=(rpb[1],))
                    if not shared:
                        b.op('vector', lambda e, _f=f4: e.tensor_tensor(
                            gsv[:, _f, :], gsv[:, _f, :], cbc, ALU.mult),
                            reads=(rxc,), writes=(rxc,))
                    pcur = Pv[ei % 2]
                    b.op('vector', lambda e, _f=f4, _p=pcur: e.tensor_tensor(
                        _p[:, _f, :], gsv[:, _f, :], pb[1][:, 0:512],
                        ALU.mult),
                        reads=(rxc, rpb[1]), writes=(rP[ei % 2],))
                if ei % 2 == 1:
                    b.op('gpsimd', lambda e, _w=src2: e.dma_start(
                        out=slots[1].rearrange('p (n o) -> p n o', o=1024),
                        in_=_w.rearrange('(n p) o -> p n o', p=128)),
                        writes=(rs_w3,), dma=True, sem='Dp2')
                    w2e_v = slots[2].rearrange('p (n o) -> p n o', o=1024)
                    w2o_v = slots[1].rearrange('p (n o) -> p n o', o=1024)
                    for dout in range(8):
                        def mmw2(e, _d=dout):
                            first = True
                            for mi, (_w2, _pi) in enumerate(
                                    ((w2e_v, 0), (w2o_v, 1))):
                                for f4 in range(4):
                                    i = e.matmul(
                                        pb[4 + _d % 2][:, 0:512],
                                        _w2[:, f4, 128 * _d:128 * _d + 128],
                                        Pv[_pi][:, f4, :],
                                        start=first,
                                        stop=(mi == 1 and f4 == 3))
                                    first = False
                            return i
                        b.op('tensor', mmw2,
                             reads=(rsl[2], rsl[1], rP[0], rP[1]),
                             writes=(self.rpb[4 + dout % 2],))
                        if first_acc[dout]:
                            b.op('vector', lambda e, _d=dout: e.tensor_copy(
                                raccv[:, _d, :], pb[4 + _d % 2][:, 0:512]),
                                reads=(self.rpb[4 + dout % 2],), writes=(rd,))
                            first_acc[dout] = False
                        else:
                            b.op('vector', lambda e, _d=dout: e.tensor_tensor(
                                raccv[:, _d, :], raccv[:, _d, :],
                                pb[4 + _d % 2][:, 0:512], ALU.add),
                                reads=(self.rpb[4 + dout % 2], rd),
                                writes=(rd,))
            self.merge_aliases()
            def ycast(e):
                for dt_ in range(8):
                    i = e.tensor_copy(xc[:, bass.ts(dt_, 512)],
                                      raccv[:, dt_, :])
                return i
            b.op('vector', ycast, reads=(rd,), writes=(rxc,))
            self.ln_resid_chunk(c4, lambda dt_: xc[:, bass.ts(dt_, 512)], rxc)

    def emit_program(self):
        self.emit_const_load()
        for bi in range(NB):
            self.load_x(bi)
            blocks = self.blocks[:1] if self.dbg == 'stop1' else self.blocks
            if self.dbg == 'b1first' and bi == 1:
                blocks = self.blocks[:1]
            for blk in blocks:
                if blk == 'l0a':
                    self.emit_l0_attn(bi)
                elif blk == 'l0f':
                    self.emit_l0_ffn(bi)
                elif blk == 'l1a':
                    self.emit_l1_attn(bi)
                elif blk == 'l1m':
                    self.emit_l1_moe(bi)
            self.store_x(bi)
        fin = [('Ds', self.b.cnt.get('Ds', 0))]
        self.b.finalize(fin)
        return self.nc


def build_program(blocks=('l0a', 'l0f', 'l1a', 'l1m')):
    k = Kern(blocks)
    return k.emit_program()


FUSED = True

def kernel(**inputs):
    prep = host_prep(inputs)
    x = np.asarray(inputs['L0_attn_x'])
    xT = np.ascontiguousarray(x.transpose(0, 2, 1))
    stage_sets = ([('l0a', 'l0f', 'l1a', 'l1m')] if FUSED
                  else [('l0a', 'l0f'), ('l1a', 'l1m')])
    cur = xT
    for blocks in stage_sets:
        nc = build_program(blocks)
        in_maps = []
        for c in range(NCORES):
            m = {'x': np.ascontiguousarray(cur[NB * c:NB * c + NB])}
            m.update(prep)
            in_maps.append(m)
        res = run_bass_kernel_spmd(nc, in_maps, core_ids=list(range(NCORES)))
        cur = np.concatenate([r['y'] for r in res.results], axis=0)
    return np.ascontiguousarray(cur.transpose(0, 2, 1))
